# version 15
# baseline (speedup 1.0000x reference)
import numpy as np
import ml_dtypes
from contextlib import ExitStack

import concourse.bass as bass
import concourse.mybir as mybir
from concourse.bass_utils import run_bass_kernel_spmd

F32 = mybir.dt.float32
BF16 = mybir.dt.bfloat16
AF = mybir.ActivationFunctionType
ALU = mybir.AluOpType
BF = ml_dtypes.bfloat16
AX = mybir.AxisListType.X

NCORES = 8
NB = 2
T = 2048
D = 2048
H = 32
DH = 64
CH = 128
NCH = T // CH
TOK = NB * T
EPS = 1e-6
MASKV = -30000.0


def _head_perm():
    p = np.concatenate([np.arange(0, DH, 2), np.arange(1, DH, 2)])
    return np.concatenate([h * DH + p for h in range(H)])


def _rope_tables():
    ROPE_BASE, SCALING, BETA_FAST, BETA_SLOW = 10000.0, 32.0, 1.0, 32.0
    half = np.arange(0, DH, 2, dtype=np.float64) / DH
    inv_freq = 1.0 / ROPE_BASE**half
    wavelen = 2.0 * np.pi / inv_freq
    ramp = np.clip((wavelen - BETA_FAST) / (BETA_SLOW - BETA_FAST), 0.0, 1.0)
    scale = 1.0 + (SCALING - 1.0) * ramp
    t = np.arange(T, dtype=np.float64)
    freqs = t[:, None] / scale[None, :] * inv_freq[None, :]
    emb = np.concatenate([freqs, freqs], axis=-1)
    return (np.cos(emb)[:, ::2].astype(np.float32),
            np.sin(emb)[:, ::2].astype(np.float32))


def host_consts():
    c = {}
    c["ident_bf"] = np.eye(128, dtype=BF)
    cos_t, sin_t = _rope_tables()
    ctab = np.zeros((NCH, 128, H * 32), np.float32)
    stab = np.zeros((NCH, 128, H * 32), np.float32)
    for n in range(NCH):
        ctab[n] = np.tile(cos_t[n * CH:(n + 1) * CH], (1, H))
        stab[n] = np.tile(sin_t[n * CH:(n + 1) * CH], (1, H))
    c["ctab"] = ctab
    c["stab"] = stab
    s = np.arange(128)[:, None]
    i = np.arange(128)[None, :]
    mts = np.where(s < i, 0.0, MASKV).astype(np.float32)
    mti = np.where(s <= i, 0.0, MASKV).astype(np.float32)
    mnat = np.where(i < s, 0.0, MASKV).astype(np.float32)
    c["masksrep"] = np.concatenate(
        [np.tile(m, (1, 4)) for m in (mts, mti, mnat)], axis=1)
    ind2f = np.zeros((2, 128), np.float32)
    ind2f[0, :64] = 1.0
    ind2f[1, 64:] = 1.0
    c["ind2f"] = ind2f
    c["ind2bf"] = ind2f.astype(BF)
    bd = np.zeros((128, 2), np.float32)
    bd[:64, 0] = 1.0
    bd[64:, 1] = 1.0
    c["indbd_bf"] = bd.astype(BF)
    fvals = np.array([1e-5, -1.0, 1.0, 1e-6, 1.0 / DH, 0.0, 0.5, 2.0],
                     np.float32)
    c["fconsts"] = np.tile(fvals, (128, 1))
    return c


_PHASE_LIMIT = None


def build_program(taps=(), phase_limit=None, scan_limit=None, scan_core=True, scan_stage=7):
    taps = set(taps)
    if phase_limit is None:
        phase_limit = _PHASE_LIMIT
    nc = bass.Bass("TRN2", target_bir_lowering=False)

    def din(name, shape, dt=F32):
        return nc.dram_tensor(name, shape, dt, kind="ExternalInput").ap()

    def dout(name, shape, dt=F32):
        return nc.dram_tensor(name, shape, dt, kind="ExternalOutput").ap()

    def dint(name, shape, dt):
        return nc.dram_tensor(name, shape, dt, kind="Internal").ap()

    x4 = din("x4", [TOK, D])
    wq = din("wq", [D, D]); wk = din("wk", [D, D]); wv = din("wv", [D, D])
    wg = din("wg", [D, D]); wo = din("wo", [D, D])
    wbgk = din("wbgk", [2 * H, D])
    wgu = din("wgu", [8 * D, D])
    wdn = din("wdn", [D, 4 * D])
    cw = {k: din("cw" + k, [D, 4]) for k in "qkv"}
    cb = {k: din("cb" + k, [D]) for k in "qkv"}
    alog = din("alog", [H]); dtb = din("dtb", [H]); bbb = din("bbb", [H])
    bgkb = din("bgkb", [H]); dvec = din("dvec", [H]); onw = din("onw", [DH])
    ident_d = din("ident_bf", [128, 128], BF16)
    ctab = din("ctab", [NCH, 128, H * 32])
    stab = din("stab", [NCH, 128, H * 32])
    masksrep_d = din("masksrep", [128, 3 * 512])
    ind2f_d = din("ind2f", [2, 128])
    ind2bf_d = din("ind2bf", [2, 128], BF16)
    indbd_d = din("indbd_bf", [128, 2], BF16)
    fconsts_d = din("fconsts", [128, 8])

    out4 = dout("out4", [TOK, D])

    w_bf = {}
    for nm, shape in [("q", [D, D]), ("k", [D, D]), ("v", [D, D]),
                      ("g", [D, D]), ("o", [D, D]), ("bgk", [2 * H, D]),
                      ("gu", [8 * D, D]), ("dn", [D, 4 * D])]:
        w_bf[nm] = dint("w_bf_" + nm, shape, BF16)
    qt = dint("qt", [D, TOK], BF16); kt = dint("kt", [D, TOK], BF16)
    vt = dint("vt", [D, TOK], BF16); gt = dint("gt", [D, TOK], BF16)
    ht = dint("ht", [4 * D, TOK], BF16)
    ott = dint("ott", [D, TOK], F32)
    ofin = dint("ofin", [D, TOK], BF16)
    crow3 = dint("crow3", [NB, 3, H, NCH, CH], BF16)
    clnb3 = dint("clnb3", [NB, 3, H, NCH, CH], BF16)
    gcs = dint("gcs", [NB, H, NCH], F32)
    bcol_d = dint("bcol_d", [H, TOK], BF16)
    nbgcol_d = dint("nbgcol_d", [H, TOK], BF16)
    gcol_d = dint("gcol_d", [H, TOK], BF16)

    tap_d = {}
    for tname, shape, dt in [
        ("xnt", [D, TOK], BF16), ("qt", [D, TOK], BF16), ("kt", [D, TOK], BF16),
        ("vt", [D, TOK], BF16), ("gt", [D, TOK], BF16),
        ("beta", [H, TOK], F32), ("gdec", [H, TOK], F32), ("chat", [H, TOK], F32),
        ("gam", [H, TOK], F32),
        ("ht", [4 * D, TOK], BF16), ("ott", [D, TOK], F32),
        ("ofin", [D, TOK], BF16),
        ("escan", [128, NCH * 128], F32), ("uscan", [128, NCH * 64], F32),
        ("sscan", [128, NCH * 256], F32), ("qktl", [128, NCH * 256], BF16),
        ("crow3", [NB * 3 * H, NCH * CH], BF16),
        ("clnb3", [NB * 3 * H, NCH * CH], BF16),
        ("bcol", [H, TOK], BF16), ("nbgcol", [H, TOK], BF16),
        ("gcol", [H, TOK], BF16), ("gcs", [NB * H, NCH], F32),
    ]:
        if tname in taps:
            tap_d[tname] = dout("tap_" + tname, shape, dt)

    ctx = ExitStack()

    def sb(name, shape, dt):
        return ctx.enter_context(nc.sbuf_tensor(name, shape, dt))

    PS = [ctx.enter_context(nc.psum_tensor(f"ps{i}", [128, 512], F32))
          for i in range(6)]
    PT0 = ctx.enter_context(nc.psum_tensor("pt0", [128, 1024], BF16))
    PT1 = ctx.enter_context(nc.psum_tensor("pt1", [128, 1024], BF16))
    PTS = [PT0, PT1]

    ident = sb("ident", [128, 128], BF16)
    fc = sb("fconsts_s", [128, 8], F32)
    for _i, _v in enumerate([1e-5, -1.0, 1.0, 1e-6, 1.0 / DH, 0.0, 0.5, 2.0]):
        nc.const_aps.register(F32, float(_v), fc[:, _i:_i + 1])
    bias64 = sb("bias64", [64, 1], F32)
    misc = sb("misc", [32, 8], F32)
    dbc = sb("dbc", [128, 16], F32)
    drows = sb("drows", [2, 16], F32)
    onwcol = sb("onwcol", [128, 1], F32)
    GCm = sb("GCm", [128, 512], F32)
    gcrows = sb("gcrows", [2, 512], F32)
    masksr = sb("masksr", [128, 3 * 512], F32)
    ind2f_s = sb("ind2f_s", [2, 128], F32)
    ind2bf_s = sb("ind2bf_s", [2, 128], BF16)
    indbd_s = sb("indbd_s", [128, 2], BF16)

    S_sb = sb("S_sb", [128, 2048], F32)
    S_bf = sb("S_bf", [128, 2048], BF16)

    blk = ctx.enter_context(nc.Block())
    ENG = ["sync", "vector", "scalar", "tensor", "gpsimd"]
    bar_sem = ctx.enter_context(nc.semaphore("bar"))
    BAR = {"n": 0}

    def barrier():
        BAR["n"] += 1
        tgt = 5 * BAR["n"]
        for en in ENG:
            def f(e, tgt=tgt):
                e.nop().then_inc(bar_sem, 1)
                e.wait_ge(bar_sem, tgt)
            getattr(blk, en)(f)

    class Ctr:
        __slots__ = ("sem", "n")

        def __init__(self, name):
            self.sem = ctx.enter_context(nc.semaphore(name))
            self.n = 0

        def fire(self, instr, k=1):
            instr.then_inc(self.sem, k)
            self.n += k
            return self.n

        def wait(self, e, v=None):
            e.wait_ge(self.sem, self.n if v is None else v)

    wcastC = {nm: Ctr("wc_" + nm) for nm in
              ("q", "bgk", "k", "v", "g", "gu", "o", "dn")}
    cld = Ctr("cld")
    xldP = [Ctr("xld0"), Ctr("xld1")]
    xsq = Ctr("xsq"); xst = Ctr("xst"); xnr = Ctr("xnr")
    xtp = Ctr("xtp"); xev = Ctr("xev")
    aux = Ctr("aux")
    tapw = Ctr("tapw")
    wldP = [Ctr("wld0"), Ctr("wld1")]
    pmm = Ctr("pmm"); pev = Ctr("pev")
    cvld = Ctr("cvld"); silu = Ctr("silu")
    pstP = [Ctr("pst0"), Ctr("pst1")]
    bp = Ctr("bp"); brow = Ctr("brow")
    guldP = [Ctr("guld0"), Ctr("guld1")]
    gumm = Ctr("gumm"); guslc = Ctr("gusl")
    guev = Ctr("guev")
    gustP = [Ctr("gust0"), Ctr("gust1")]
    scldP = [Ctr("scld0"), Ctr("scld1")]
    sctp = Ctr("sctp"); scev = Ctr("scev")
    scv = Ctr("scv"); scs = Ctr("scs"); gcev = Ctr("gcev")
    dmm = Ctr("dmm"); emk = Ctr("emk"); gmm = Ctr("gmm"); nbld = Ctr("nbld")
    slv = Ctr("slv"); sev = Ctr("sev"); ymm = Ctr("ymm"); rbld = Ctr("rbld")
    amm = Ctr("amm"); aev = Ctr("aev"); omm = Ctr("omm"); oev = Ctr("oev")
    smm = Ctr("smm"); supd = Ctr("supd")
    ostP = [Ctr("ost0"), Ctr("ost1")]
    tapE = Ctr("tapE")
    p3ldP = [Ctr("p3ld0"), Ctr("p3ld1")]
    p3v = Ctr("p3v"); p3s = Ctr("p3s"); p3m = Ctr("p3m")
    p3stP = [Ctr("p3st0"), Ctr("p3st1")]
    p4wl = Ctr("p4wl")
    p4alP = [Ctr("p4al0"), Ctr("p4al1")]
    p4mm = Ctr("p4mm"); p4ev = Ctr("p4ev")
    outwP = [Ctr("outw0"), Ctr("outw1")]

    stack_a = ExitStack()

    def sba(name, shape, dt):
        return stack_a.enter_context(nc.sbuf_tensor(name, shape, dt))

    XNT = sba("XNT", [128, 16 * TOK], BF16)
    XNTr = XNT[:].rearrange("p (a c) -> p a c", a=16)
    stack_x = ExitStack()

    def sbx(name, shape, dt):
        return stack_x.enter_context(nc.sbuf_tensor(name, shape, dt))

    xbuf = [sbx(f"xbuf{i}", [128, D], F32) for i in range(2)]
    xnbf = [sbx(f"xnbf{i}", [128, D], BF16) for i in range(2)]
    stat = [sbx(f"stat{i}", [128, 32], F32) for i in range(2)]

    CASTDONE = {}

    def ph0_gp(e):
        order = [("q", wq, D, D), ("bgk", wbgk, 2 * H, D), ("k", wk, D, D),
                 ("v", wv, D, D), ("g", wg, D, D), ("gu", wgu, 8 * D, D),
                 ("o", wo, D, D), ("dn", wdn, D, 4 * D)]
        for nm, src, rows, width in order:
            if nm == "gu":
                for gg in range(32):
                    wcastC[nm].fire(e.dma_start(
                        out=w_bf[nm][gg * 512:gg * 512 + 256, :],
                        in_=src[gg * 256:(gg + 1) * 256, :]), 16)
                    wcastC[nm].fire(e.dma_start(
                        out=w_bf[nm][gg * 512 + 256:(gg + 1) * 512, :],
                        in_=src[4 * D + gg * 256:4 * D + (gg + 1) * 256, :]), 16)
                CASTDONE[nm] = wcastC[nm].n
                continue
            step = max(1, (4 << 20) // (width * 4))
            r = 0
            while r < rows:
                rr = min(step, rows - r)
                wcastC[nm].fire(e.dma_start(out=w_bf[nm][r:r + rr, :],
                                            in_=src[r:r + rr, :]), 16)
                r += rr
            CASTDONE[nm] = wcastC[nm].n
    blk.gpsimd(ph0_gp)

    def ph0_sync(e):
        def ld(dst, src):
            cld.fire(e.dma_start(out=dst, in_=src), 16)
        ld(ident[:], ident_d[:, :])
        ld(fc[:], fconsts_d[:, :])
        ld(masksr[:], masksrep_d[:, :])
        ld(ind2f_s[:], ind2f_d[:, :])
        ld(ind2bf_s[:], ind2bf_d[:, :])
        ld(indbd_s[:], indbd_d[:, :])
        ld(bias64[0:32, :], bbb[:].rearrange("(p o) -> p o", o=1))
        ld(bias64[32:64, :], bgkb[:].rearrange("(p o) -> p o", o=1))
        ld(misc[:, 0:1], dtb[:].rearrange("(p o) -> p o", o=1))
        ld(misc[:, 1:2], alog[:].rearrange("(p o) -> p o", o=1))
        ld(onwcol[0:64, :], onw[:].rearrange("(p o) -> p o", o=1))
        ld(onwcol[64:128, :], onw[:].rearrange("(p o) -> p o", o=1))
        cld.fire(e.dma_start(out=drows[:, :],
                             in_=dvec[:].rearrange("(j t) -> t j", t=2),
                             allow_slow_non_contiguous=True), 16)
        for tt in range(32):
            if tt >= 2:
                xnr.wait(e, tt - 1)
            xldP[tt % 2].fire(e.dma_start(out=xbuf[tt % 2][:],
                                          in_=x4[tt * 128:(tt + 1) * 128, :]), 16)
    blk.sync(ph0_sync)
    CLD_TOTAL = cld.n

    def ph0_scal(e):
        e.wait_ge(cld.sem, CLD_TOTAL)
        for tt in range(32):
            st = stat[tt % 2]
            e.wait_ge(xst.sem, 3 * tt + 2)
            xsq.fire(e.activation(st[:, 2:3], st[:, 1:2], AF.Sqrt, bias=1e-5))
            for g in range(2):
                e.wait_ge(xtp.sem, 2 * tt + g + 1)
                dst = XNTr[:, 8 * g:8 * (g + 1), tt * 128:(tt + 1) * 128]
                xev.fire(e.copy(
                    dst, PTS[g][:, 0:1024].rearrange("p (a c) -> p a c", a=8)))
        xsq.fire(e.activation(misc[:, 3:4], misc[:, 1:2], AF.Exp))
    blk.scalar(ph0_scal)
    XSQ_EA = xsq.n

    def ph0_vec(e):
        e.wait_ge(cld.sem, CLD_TOTAL)
        for tt in range(32):
            st = stat[tt % 2]
            xb = xbuf[tt % 2]
            e.wait_ge(xldP[tt % 2].sem, 16 * (tt // 2 + 1))
            if tt >= 2:
                e.wait_ge(xsq.sem, tt - 1)
            for j in range(4):
                ii = e.bn_stats(st[:, 8 + 6 * j:8 + 6 * (j + 1)],
                                xb[:, 512 * j:512 * (j + 1)])
            xst.fire(ii)
            e.wait_ge(xst.sem, 3 * tt + 1)
            xst.fire(e.bn_aggr(st[:, 0:2], st[:, 8:32]))
            e.wait_ge(xsq.sem, tt + 1)
            xst.fire(e.reciprocal(st[:, 3:4], st[:, 2:3]))
            e.wait_ge(xst.sem, 3 * tt + 3)
            if tt >= 2:
                e.wait_ge(xtp.sem, 2 * (tt - 1))
            xnr.fire(e.tensor_scalar(xnbf[tt % 2][:], xb[:], st[:, 0:1],
                                     st[:, 3:4], ALU.subtract, ALU.mult))
        e.wait_ge(xsq.sem, XSQ_EA)
        e.tensor_scalar(misc[:, 2:3], misc[:, 3:4], -1.0, None, ALU.mult)
    blk.vector(ph0_vec)

    def ph0_tens(e):
        e.wait_ge(cld.sem, CLD_TOTAL)
        aux.fire(e.matmul(PS[0][:, 0:16], ind2f_s[:], drows[:, :],
                          start=True, stop=True))
        for tt in range(32):
            e.wait_ge(xnr.sem, tt + 1)
            for g in range(2):
                if tt >= 1:
                    e.wait_ge(xev.sem, 2 * (tt - 1) + g + 1)
                for j in range(8):
                    dblk = 8 * g + j
                    mm = e.transpose(PTS[g][:, j * 128:(j + 1) * 128],
                                     xnbf[tt % 2][:, dblk * 128:(dblk + 1) * 128],
                                     ident[:])
                xtp.fire(mm)
    blk.tensor(ph0_tens)
    AUX_DBC = aux.n

    def ph0_vec2(e):
        e.wait_ge(aux.sem, AUX_DBC)
        aux.fire(e.tensor_copy(dbc[:], PS[0][:, 0:16]))
    blk.vector(ph0_vec2)

    barrier()

    if "xnt" in tap_d:
        def f(e):
            for dblk in range(16):
                tapw.fire(e.dma_start(out=tap_d["xnt"][dblk * 128:(dblk + 1) * 128, :],
                                      in_=XNT[:, dblk * TOK:(dblk + 1) * TOK]), 16)
        blk.gpsimd(f)
        barrier()

    if phase_limit == 1:
        stack_x.close()
        stack_a.close()
        ctx.close()
        return nc
    stack_x.close()

    stack_p = ExitStack()

    def sbp(name, shape, dt):
        return stack_p.enter_context(nc.sbuf_tensor(name, shape, dt))

    wsl = [sbp(f"wsl{i}", [128, 16 * 512], BF16) for i in range(2)]
    cvw = sbp("cvw", [128, 4], F32)
    cvb = sbp("cvb", [128, 1], F32)
    convst = [sbp(f"convst{i}", [128, 515], F32) for i in range(2)]
    convy = [sbp(f"convy{i}", [128, 512], F32) for i in range(2)]
    pstg = [sbp(f"pstg{i}", [128, 512], BF16) for i in range(2)]

    def proj_weight(nm, conv_w, conv_b, dst_scr):
        is_conv = nm in ("q", "k", "v")
        nrows = 64 if nm == "bgk" else D
        n4 = max(1, nrows // 512)
        nsub = 4 if nm != "bgk" else 1
        msize = 128 if nm != "bgk" else 64
        wfree_need = {}
        pev_slot = {}
        tile_idx = [0]
        st_cnt = [0]
        sl_cnt = [0]
        silu_base = silu.n
        pst_base = [pstP[0].n, pstP[1].n]
        prev_pmm = pmm.n

        for j4 in range(n4):
            wid = 512 if nm != "bgk" else 64

            def f_sync(e, j4=j4, wid=wid):
                e.wait_ge(wcastC[nm].sem, CASTDONE[nm])
                if j4 >= 2:
                    e.wait_ge(pmm.sem, wfree_need[j4 - 2])
                else:
                    e.wait_ge(pmm.sem, prev_pmm)
                for dblk in range(16):
                    src = w_bf[nm][j4 * 512:j4 * 512 + wid,
                                   dblk * 128:(dblk + 1) * 128]
                    wldP[j4 % 2].fire(
                        e.dma_start(out=wsl[j4 % 2][:, dblk * 512:dblk * 512 + wid],
                                    in_=src, transpose=True), 16)
            blk.sync(f_sync)
            wld_after = wldP[j4 % 2].n
            wld_par = j4 % 2

            for ns in range(nsub):
                ntile = j4 * 4 + ns
                if is_conv:
                    def f_syncc(e, ntile=ntile):
                        e.wait_ge(silu.sem, silu_base + 8 * ntile)
                        cvld.fire(e.dma_start(
                            out=cvw[:], in_=conv_w[ntile * 128:(ntile + 1) * 128, :]), 16)
                        cvld.fire(e.dma_start(
                            out=cvb[:], in_=conv_b[ntile * 128:(ntile + 1) * 128]
                            .rearrange("(p o) -> p o", o=1)), 16)
                    blk.sync(f_syncc)
                cvld_now = cvld.n

                pev_base = pev.n

                def f_tens(e, j4=j4, ns=ns, wld_after=wld_after, msize=msize,
                           pev_base=pev_base):
                    e.wait_ge(wldP[wld_par].sem, wld_after)
                    for tk in range(8):
                        slot = (ns * 8 + tk) % 4
                        if tk >= 4:
                            e.wait_ge(pev.sem, pev_base + tk - 3)
                        else:
                            need = pev_slot.get(slot)
                            if need:
                                e.wait_ge(pev.sem, need)
                        for dblk in range(16):
                            lhs = wsl[j4 % 2][:, dblk * 512 + ns * msize:
                                              dblk * 512 + (ns + 1) * msize]
                            rhs = XNTr[:, dblk, tk * 512:(tk + 1) * 512]
                            mm = e.matmul(PS[slot][0:msize, :], lhs, rhs,
                                          start=(dblk == 0), stop=(dblk == 15))
                        pmm.fire(mm)
                blk.tensor(f_tens)

                for tk in range(8):
                    slot = (ns * 8 + tk) % 4
                    pmm_need = pmm.n - 8 + tk + 1
                    ti = tile_idx[0]
                    if is_conv:
                        def f_vec(e, tk=tk, slot=slot, pmm_need=pmm_need,
                                  cvld_now=cvld_now, ti=ti):
                            par = ti % 2
                            e.wait_ge(pmm.sem, pmm_need)
                            e.wait_ge(cvld.sem, cvld_now)
                            if ti >= 2:
                                e.wait_ge(silu.sem, silu_base + sl_cnt[0] - 1)
                            e.tensor_copy(convst[par][:, 3:515], PS[slot][:, :])
                            if tk in (0, 4):
                                e.memset(convst[par][:, 0:3], 0.0)
                            else:
                                e.tensor_copy(convst[par][:, 0:3],
                                              convst[1 - par][:, 512:515])
                            e.tensor_scalar(convy[par][:], convst[par][:, 0:512],
                                            cvw[:, 0:1], cvb[:, 0:1],
                                            ALU.mult, ALU.add)
                            for jj in range(1, 4):
                                e.scalar_tensor_tensor(
                                    convy[par][:], convst[par][:, jj:jj + 512],
                                    cvw[:, jj:jj + 1], convy[par][:],
                                    ALU.mult, ALU.add)
                            pev.fire(e.engine_nop())
                        blk.vector(f_vec)
                        pev_after = pev.n
                        pev_slot[slot] = pev_after

                        def f_scal(e, ti=ti, pev_after=pev_after):
                            e.wait_ge(pev.sem, pev_after)
                            if ti >= 2:
                                e.wait_ge(pstP[ti % 2].sem,
                                          pst_base[ti % 2] + 16 * (ti // 2))
                            silu.fire(e.activation(pstg[ti % 2][:],
                                                   convy[ti % 2][:], AF.Silu))
                        blk.scalar(f_scal)
                        sl_cnt[0] += 1
                        silu_after = silu.n

                        def f_gp(e, ntile=ntile, tk=tk, ti=ti, silu_after=silu_after):
                            e.wait_ge(silu.sem, silu_after)
                            pstP[ti % 2].fire(e.dma_start(
                                out=dst_scr[ntile * 128:(ntile + 1) * 128,
                                            tk * 512:(tk + 1) * 512],
                                in_=pstg[ti % 2][:]), 16)
                        blk.gpsimd(f_gp)
                        st_cnt[0] += 1
                    elif nm == "g":
                        def f_scal(e, tk=tk, slot=slot, pmm_need=pmm_need, ti=ti):
                            e.wait_ge(pmm.sem, pmm_need)
                            if ti >= 2:
                                e.wait_ge(pstP[ti % 2].sem,
                                          pst_base[ti % 2] + 16 * (ti // 2))
                            pev.fire(e.activation(pstg[ti % 2][:], PS[slot][:, :],
                                                  AF.Identity))
                        blk.scalar(f_scal)
                        pev_slot[slot] = pev.n
                        pev_after = pev.n

                        def f_gp(e, ntile=ntile, tk=tk, ti=ti, pev_after=pev_after):
                            e.wait_ge(pev.sem, pev_after)
                            pstP[ti % 2].fire(e.dma_start(
                                out=dst_scr[ntile * 128:(ntile + 1) * 128,
                                            tk * 512:(tk + 1) * 512],
                                in_=pstg[ti % 2][:]), 16)
                        blk.gpsimd(f_gp)
                        st_cnt[0] += 1
                    else:
                        raise AssertionError("unused branch")
                    tile_idx[0] += 1

            wfree_need[j4] = pmm.n

    proj_weight("q", cw["q"], cb["q"], qt)
    proj_weight("k", cw["k"], cb["k"], kt)
    proj_weight("v", cw["v"], cb["v"], vt)
    proj_weight("g", None, None, gt)

    stack_p.close()

    stack_bk = ExitStack()

    def sbk(name, shape, dt):
        return stack_bk.enter_context(nc.sbuf_tensor(name, shape, dt))

    wslb = sbk("wslb", [128, 16 * 64], BF16)
    bb_col = sbk("bb_col", [32, 1], F32)
    gkb_col = sbk("gkb_col", [32, 1], F32)
    ones512 = sbk("ones512", [32, 512], F32)
    c_carry = sbk("c_carry", [32, 1], F32)
    tPB = sbk("tPB", [32, 512], F32)
    tPG = sbk("tPG", [32, 512], F32)
    tB = sbk("tB", [32, 512], F32)
    tLNB = sbk("tLNB", [32, 512], F32)
    t1 = sbk("t1", [32, 512], F32)
    t2 = sbk("t2", [32, 512], F32)
    t3 = sbk("t3", [32, 512], F32)
    tG = sbk("tG", [32, 512], F32)
    tC = sbk("tC", [32, 512], F32)
    tCH = sbk("tCH", [32, 512], F32)
    tGA = sbk("tGA", [32, 512], F32)
    tCLN = sbk("tCLN", [32, 512], F32)
    tNBG = sbk("tNBG", [32, 512], F32)
    bfb = sbk("bfb", [32, 512], BF16)
    bfnbg = sbk("bfnbg", [32, 512], BF16)
    bfg = sbk("bfg", [32, 512], BF16)
    cmp3 = [[sbk(f"cmp3_{v}_{j}", [32, 512], BF16) for j in range(3)]
            for v in range(2)]
    gsampc = sbk("gsampc", [32, 4], F32)

    def bgk_sync(e):
        e.wait_ge(wcastC["bgk"].sem, CASTDONE["bgk"])
        e.wait_ge(pmm.sem, pmm.n)
        e.wait_ge(pev.sem, pev.n)
        e.wait_ge(silu.sem, silu.n)
        e.wait_ge(pstP[0].sem, pstP[0].n)
        e.wait_ge(pstP[1].sem, pstP[1].n)
        for dblk in range(16):
            wldP[0].fire(e.dma_start(out=wslb[:, dblk * 64:(dblk + 1) * 64],
                                     in_=w_bf["bgk"][0:64,
                                                     dblk * 128:(dblk + 1) * 128],
                                     transpose=True), 16)
        cld.fire(e.dma_start(out=bb_col[:],
                             in_=bbb[:].rearrange("(p o) -> p o", o=1)), 16)
        cld.fire(e.dma_start(out=gkb_col[:],
                             in_=bgkb[:].rearrange("(p o) -> p o", o=1)), 16)
    blk.sync(bgk_sync)
    wld_bgk = wldP[0].n
    cld_bgk = cld.n

    def bgk_vinit(e):
        e.wait_ge(cld.sem, cld_bgk)
        e.wait_ge(xsq.sem, XSQ_EA)
        e.memset(ones512[:], 1.0)
        bp.fire(e.tensor_tensor(gkb_col[:], gkb_col[:], misc[0:32, 0:1], ALU.add))
    blk.vector(bgk_vinit)

    for ch in range(8):
        bloc = ch // 4
        n0 = (ch % 4) * 4
        cs = slice(ch * 512, (ch + 1) * 512)
        slotA = ch % 2
        slotB = 2 + ch % 2
        bp_prev = bp.n

        if ch == 0:
            bgk_pev = {}

        def bgk_tens(e, ch=ch, slotA=slotA, slotB=slotB):
            e.wait_ge(wldP[0].sem, wld_bgk)
            if ch >= 2:
                e.wait_ge(bp.sem, bgk_pev[ch - 2])
            for lo, slot in ((0, slotA), (32, slotB)):
                for dblk in range(16):
                    lhs = wslb[:, dblk * 64 + lo:dblk * 64 + lo + 32]
                    rhs = XNTr[:, dblk, ch * 512:(ch + 1) * 512]
                    mm = e.matmul(PS[slot][0:32, :], lhs, rhs,
                                  start=(dblk == 0), stop=(dblk == 15))
                pmm.fire(mm)
        blk.tensor(bgk_tens)
        pmm_bgk = pmm.n

        def b_s1(e, ch=ch, slotA=slotA, slotB=slotB, pmm_bgk=pmm_bgk):
            e.wait_ge(pmm.sem, pmm_bgk)
            e.wait_ge(bp.sem, bp.n)
            e.activation(tPB[:], PS[slotA][0:32, :], AF.Identity,
                         bias=bb_col[:, 0:1])
            bp.fire(e.activation(tPG[:], PS[slotB][0:32, :], AF.Identity,
                                 bias=gkb_col[:, 0:1]))
            e.wait_ge(bp.sem, bp.n)
            e.activation(tB[:], tPB[:], AF.Sigmoid)
            e.activation(t1[:], tPB[:], AF.Relu, scale=-1.0)
            bp.fire(e.activation(t2[:], tPB[:], AF.Abs))
            e.wait_ge(bp.sem, bp.n)
            bp.fire(e.activation(t3[:], t2[:], AF.Exp, scale=-1.0))
            e.wait_ge(bp.sem, bp.n)
            bp.fire(e.activation(t2[:], t3[:], AF.Ln, bias=1.0))
        blk.scalar(b_s1)
        bgk_pev[ch] = bp.n

        def b_v1(e):
            e.wait_ge(bp.sem, bp.n)
            bp.fire(e.scalar_tensor_tensor(tLNB[:], t2[:], -1.0, t1[:],
                                           ALU.mult, ALU.subtract))
        blk.vector(b_v1)

        def b_s2(e):
            e.wait_ge(bp.sem, bp.n)
            e.activation(t1[:], tPG[:], AF.Relu)
            bp.fire(e.activation(t2[:], tPG[:], AF.Abs))
            e.wait_ge(bp.sem, bp.n)
            bp.fire(e.activation(t3[:], t2[:], AF.Exp, scale=-1.0))
            e.wait_ge(bp.sem, bp.n)
            bp.fire(e.activation(t2[:], t3[:], AF.Ln, bias=1.0))
        blk.scalar(b_s2)

        def b_v2(e, ch=ch):
            e.wait_ge(bp.sem, bp.n)
            e.tensor_add(t3[:], t2[:], t1[:])
            e.tensor_scalar(tG[:], t3[:], misc[0:32, 2:3], None, ALU.mult)
            if ch % 4 == 0:
                bp.fire(e.tensor_tensor_scan(tC[:], ones512[:], tG[:], 0.0,
                                             ALU.mult, ALU.add))
            else:
                bp.fire(e.tensor_tensor_scan(tC[:], ones512[:], tG[:],
                                             c_carry[:, 0:1],
                                             ALU.mult, ALU.add))
            e.wait_ge(bp.sem, bp.n)
            if ch % 4 == 0:
                e.tensor_copy(tCH[:, 0:128], tC[:, 0:128])
            else:
                e.tensor_scalar(tCH[:, 0:128], tC[:, 0:128],
                                c_carry[:, 0:1], None, ALU.subtract)
            e.tensor_copy(c_carry[:], tC[:, 511:512])
            for j in range(1, 4):
                e.tensor_scalar(tCH[:, j * 128:(j + 1) * 128],
                                tC[:, j * 128:(j + 1) * 128],
                                tC[:, j * 128 - 1:j * 128], None,
                                ALU.subtract)
            bp.fire(e.engine_nop())
        blk.vector(b_v2)

        def b_s3(e):
            e.wait_ge(bp.sem, bp.n)
            bp.fire(e.activation(tGA[:], tCH[:], AF.Exp))
        blk.scalar(b_s3)

        def b_v3(e):
            e.wait_ge(bp.sem, bp.n)
            e.tensor_add(tCLN[:], tCH[:], tLNB[:])
            e.scalar_tensor_tensor(tNBG[:], tB[:], -1.0, tGA[:],
                                   ALU.mult, ALU.mult)
            e.tensor_copy(bfb[:], tB[:])
            e.tensor_copy(bfnbg[:], tNBG[:])
            e.tensor_copy(bfg[:], tGA[:])
            e.tensor_copy(gsampc[:],
                          tGA[:, :].rearrange("p (n c) -> p n c", c=128)[:, :, 127:128]
                          .rearrange("p n o -> p (n o)"))
            for v, src in ((0, tCH), (1, tCLN)):
                e.tensor_copy(cmp3[v][0][:], src[:])
                e.tensor_sub(t1[:], src[:], cmp3[v][0][:])
                e.tensor_copy(cmp3[v][1][:], t1[:])
                e.tensor_sub(t2[:], t1[:], cmp3[v][1][:])
                e.tensor_copy(cmp3[v][2][:], t2[:])
            bp.fire(e.engine_nop())
        blk.vector(b_v3)
        bv3_done = bp.n

        def b_gp(e, bloc=bloc, n0=n0, cs=cs, bv3_done=bv3_done):
            e.wait_ge(bp.sem, bv3_done)
            for v, scr3 in ((0, crow3), (1, clnb3)):
                for j in range(3):
                    brow.fire(e.dma_start(
                        out=scr3[bloc:bloc + 1, j:j + 1, :, n0:n0 + 4, :]
                        .rearrange("o oo h m c -> (o oo h) (m c)"),
                        in_=cmp3[v][j][:]), 16)
            brow.fire(e.dma_start(
                out=gcs[bloc:bloc + 1, :, n0:n0 + 4]
                .rearrange("o h m -> (o h) m"), in_=gsampc[:]), 16)
            brow.fire(e.dma_start(out=bcol_d[:, cs], in_=bfb[:]), 16)
            brow.fire(e.dma_start(out=nbgcol_d[:, cs], in_=bfnbg[:]), 16)
            brow.fire(e.dma_start(out=gcol_d[:, cs], in_=bfg[:]), 16)
        blk.gpsimd(b_gp)
        brow_done = brow.n

        def b_guard(e, brow_done=brow_done):
            e.wait_ge(brow.sem, brow_done)
        blk.scalar(b_guard)

    def drain_p(e):
        e.wait_ge(pstP[0].sem, pstP[0].n)
        e.wait_ge(pstP[1].sem, pstP[1].n)
        e.wait_ge(brow.sem, brow.n)
    blk.gpsimd(drain_p)
    barrier()

    for tn, src in [("qt", qt), ("kt", kt), ("vt", vt), ("gt", gt)]:
        if tn in tap_d:
            def f(e, tn=tn, src=src):
                for r in range(0, D, 512):
                    tapw.fire(e.dma_start(out=tap_d[tn][r:r + 512, :],
                                          in_=src[r:r + 512, :]), 16)
            blk.gpsimd(f)
    for tn, src, rr in [("crow3", crow3, NB * 3 * H), ("clnb3", clnb3, NB * 3 * H),
                        ("bcol", bcol_d, H), ("nbgcol", nbgcol_d, H),
                        ("gcol", gcol_d, H), ("gcs", gcs, NB * H)]:
        if tn in tap_d:
            def f(e, tn=tn, src=src, rr=rr):
                tapw.fire(e.dma_start(
                    out=tap_d[tn][:, :],
                    in_=src[:].rearrange("... -> ...") if False else
                    src.rearrange("a b c d e -> (a b c) (d e)") if tn in ("crow3", "clnb3")
                    else (src.rearrange("a b c -> (a b) c") if tn == "gcs"
                          else src[:, :])), 16)
            blk.gpsimd(f)
    if tap_d:
        barrier()

    if phase_limit == 2:
        stack_bk.close()
        stack_a.close()
        ctx.close()
        return nc
    stack_bk.close()

    gusl = [sba(f"gusl{i}", [128, 16 * 512], BF16) for i in range(2)]
    sstg = [sba(f"sstg{i}", [128, 512], F32) for i in range(2)]
    hstg = [sba(f"hstg{i}", [128, 512], BF16) for i in range(2)]

    gufree_need = {}
    guev_slot = {}
    gu_idx = [0]

    for jg in range(32):
        def f_sync(e, jg=jg):
            e.wait_ge(wcastC["gu"].sem, CASTDONE["gu"])
            if jg >= 2:
                e.wait_ge(gumm.sem, gufree_need[jg - 2])
            for dblk in range(16):
                guldP[jg % 2].fire(e.dma_start(
                    out=gusl[jg % 2][:, dblk * 512:(dblk + 1) * 512],
                    in_=w_bf["gu"][jg * 512:(jg + 1) * 512,
                                   dblk * 128:(dblk + 1) * 128],
                    transpose=True), 16)
        blk.sync(f_sync)
        guld_after = guldP[jg % 2].n

        for sj in range(2):
            for tk in range(8):
                idx = gu_idx[0]
                slot_g = (2 * idx) % 4
                slot_u = (2 * idx + 1) % 4

                def f_tens(e, jg=jg, sj=sj, tk=tk, guld_after=guld_after,
                           slot_g=slot_g, slot_u=slot_u):
                    e.wait_ge(guldP[jg % 2].sem, guld_after)
                    for sl in (slot_g, slot_u):
                        need = guev_slot.get(sl)
                        if need:
                            e.wait_ge(guev.sem, need)
                    for off, slot in ((0, slot_g), (256, slot_u)):
                        for dblk in range(16):
                            lhs = gusl[jg % 2][:, dblk * 512 + off + sj * 128:
                                               dblk * 512 + off + (sj + 1) * 128]
                            rhs = XNTr[:, dblk, tk * 512:(tk + 1) * 512]
                            mm = e.matmul(PS[slot][:, :], lhs, rhs,
                                          start=(dblk == 0), stop=(dblk == 15))
                        gumm.fire(mm)
                blk.tensor(f_tens)
                gumm_gate = gumm.n - 1
                gumm_all = gumm.n

                def f_scal(e, idx=idx, slot_g=slot_g, gumm_gate=gumm_gate):
                    e.wait_ge(gumm.sem, gumm_gate)
                    if idx >= 2:
                        e.wait_ge(guev.sem, 2 * (idx - 1))
                    gusl_i = e.activation(sstg[idx % 2][:], PS[slot_g][:, :],
                                          AF.Silu)
                    guslc.fire(gusl_i)
                blk.scalar(f_scal)
                gusl_after = guslc.n

                def f_vec(e, idx=idx, slot_u=slot_u, gumm_all=gumm_all,
                          gusl_after=gusl_after):
                    e.wait_ge(guslc.sem, gusl_after)
                    e.wait_ge(gumm.sem, gumm_all)
                    if idx >= 2:
                        e.wait_ge(gustP[idx % 2].sem, 16 * (idx // 2))
                    guev.fire(e.tensor_tensor(hstg[idx % 2][:], sstg[idx % 2][:],
                                              PS[slot_u][:, :], ALU.mult), 2)
                blk.vector(f_vec)
                guev_slot[slot_g] = guev.n
                guev_slot[slot_u] = guev.n
                guev_after = guev.n

                def f_gp(e, jg=jg, sj=sj, tk=tk, idx=idx, guev_after=guev_after):
                    e.wait_ge(guev.sem, guev_after)
                    row = jg * 256 + sj * 128
                    gustP[idx % 2].fire(e.dma_start(
                        out=ht[row:row + 128, tk * 512:(tk + 1) * 512],
                        in_=hstg[idx % 2][:]), 16)
                blk.gpsimd(f_gp)
                gu_idx[0] += 1

        gufree_need[jg] = gumm.n

    def drain_gu(e):
        e.wait_ge(gustP[0].sem, gustP[0].n)
        e.wait_ge(gustP[1].sem, gustP[1].n)
    blk.gpsimd(drain_gu)
    barrier()
    if "ht" in tap_d:
        def f(e):
            for r in range(0, 4 * D, 512):
                tapw.fire(e.dma_start(out=tap_d["ht"][r:r + 512, :],
                                      in_=ht[r:r + 512, :]), 16)
        blk.gpsimd(f)
        barrier()

    if phase_limit == 3:
        stack_a.close()
        ctx.close()
        return nc
    stack_a.close()

    stack_b = ExitStack()

    def sbb(name, shape, dt):
        return stack_b.enter_context(nc.sbuf_tensor(name, shape, dt))

    qsl = [sbb(f"qsl{i}", [128, 2048], BF16) for i in range(2)]
    ksl = [sbb(f"ksl{i}", [128, 2048], BF16) for i in range(2)]
    vsl = [sbb(f"vsl{i}", [128, 2048], BF16) for i in range(2)]
    ct_sb = sbb("ct_sb", [128, 1024], F32)
    st_sb = sbb("st_sb", [128, 1024], F32)
    colb = [sbb(f"colb{i}", [128, 96], BF16) for i in range(2)]
    colbF = [sbb(f"colbF{i}", [128, 96], F32) for i in range(2)]
    T_A = sbb("T_A", [35, 4096], BF16)
    T_B = sbb("T_B", [35, 4096], BF16)
    T_C = sbb("T_C", [35, 4096], BF16)
    qnat = sbb("qnat", [128, 2048], F32)
    knat = sbb("knat", [128, 2048], F32)
    vnat = sbb("vnat", [128, 2048], BF16)
    vb_bf = sbb("vb_bf", [128, 2048], BF16)
    qrp = sbb("qrp", [128, 2048], F32)
    krp = sbb("krp", [128, 2048], F32)
    rsums = sbb("rsums", [128, 64], F32)
    rsq2 = sbb("rsq2", [128, 64], F32)
    sclq = sbb("sclq", [128, 32], F32)
    qtil = sbb("qtil", [128, 2048], BF16)
    ktil = sbb("ktil", [128, 2048], BF16)
    kpr = sbb("kpr", [128, 2048], BF16)
    qT2 = sbb("qT2", [128, 2048], BF16)
    kT2 = sbb("kT2", [128, 2048], BF16)
    EPs = [sbb(f"EPs{i}", [128, 512], F32) for i in range(2)]
    ENs = [sbb(f"ENs{i}", [128, 512], F32) for i in range(2)]
    ENns = [sbb(f"ENns{i}", [128, 512], F32) for i in range(2)]
    PTbs = [sbb(f"PTbs{i}", [128, 512], BF16) for i in range(2)]
    trsS = [[sbb(f"trsS{p}_{j}", [128, 512], BF16) for j in range(2)]
            for p in range(2)]
    natS = [[sbb(f"natS{p}_{j}", [128, 512], BF16) for j in range(2)]
            for p in range(2)]
    Rps = [[sbb(f"Rps{p}_{j}", [128, 256], BF16) for j in range(2)]
           for p in range(2)]
    Ubs = [sbb(f"Ubs{i}", [128, 256], BF16) for i in range(2)]
    zrow = sbb("zrow", [1, 256], BF16)
    ostg = [sbb(f"ostg{i}", [128, 2048], F32) for i in range(2)]

    def sc_init_v(e):
        e.memset(zrow[:], 0.0)
        e.memset(S_sb[:], 0.0)
        e.memset(S_bf[:], 0.0)
        e.memset(T_A[:], 0.0)
        e.memset(T_B[:], 0.0)
        e.memset(T_C[:], 0.0)
        e.memset(T_A[0:3, :], 1.0)
        e.memset(T_B[32:35, :], -1.0)
        e.memset(T_C[32:35, :], -1.0)
        scv.fire(e.engine_nop())
    blk.vector(sc_init_v)
    SCV_INIT = scv.n

    def sc_init_sync(e):
        scldP[0].fire(e.dma_start(
            out=gcrows[:, :].rearrange("t (b hp n) -> t b hp n", b=NB, hp=16),
            in_=gcs[:, :, :].rearrange("b (hp t) n -> t b hp n", t=2)), 16)
    blk.sync(sc_init_sync)
    SCLD_GC = scldP[0].n

    def sc_init_t(e):
        e.wait_ge(scldP[0].sem, SCLD_GC)
        gcev.fire(e.matmul(PS[5][:, :], ind2f_s[:], gcrows[:, :],
                           start=True, stop=True))
    blk.tensor(sc_init_t)
    GC_MM = gcev.n

    def sc_init_v2(e):
        e.wait_ge(gcev.sem, GC_MM)
        gcev.fire(e.tensor_copy(GCm[:], PS[5][:, :]))
    blk.vector(sc_init_v2)
    GC_DONE = gcev.n

    sctp_fin = {}
    supd_fin = {-1: 0, 0: 0}
    dmm_fin = {}
    ost_fin = {}
    scvtil_fin = {}
    scvrope_fin = {}
    rbldit_fin = {}
    nbld_fin = {}
    omm_fin = {}
    amm_fin = {}
    ps_guard = {i: None for i in range(6)}

    def guard(e, bank):
        g = ps_guard[bank]
        if g is not None:
            e.wait_ge(g[0].sem, g[1])

    _sc_pairs = [(n, b) for n in range(NCH) for b in range(NB)]
    if scan_limit is not None:
        _sc_pairs = _sc_pairs[:scan_limit]

    def _scan_gen(n, b, it):
        if True:
            par = it % 2
            base = b * T + n * CH

            def f_sync(e):
                if it >= 2:
                    e.wait_ge(sctp.sem, sctp_fin[it - 2])
                    e.wait_ge(supd.sem, supd_fin[it - 2])
                sc = scldP[par]
                for scr, dst in ((qt, qsl[par]), (kt, ksl[par]), (vt, vsl[par])):
                    for a4 in range(4):
                        src = scr[a4 * 512:(a4 + 1) * 512, base:base + CH]\
                            .rearrange("(a p) c -> p a c", p=128)
                        sc.fire(e.dma_start(
                            out=dst[:, a4 * 512:(a4 + 1) * 512]
                            .rearrange("p (a c) -> p a c", a=4), in_=src), 16)
                for j, scr in enumerate((bcol_d, nbgcol_d, gcol_d)):
                    sc.fire(e.dma_start(out=colb[par][:, j * 32:(j + 1) * 32],
                                        in_=scr[:, base:base + CH],
                                        transpose=True), 16)
                if b == 0:
                    e.wait_ge(scv.sem, scvrope_fin.get(it - 1, 0))
                    sc.fire(e.dma_start(out=ct_sb[:], in_=ctab[n, :, :]), 16)
                    sc.fire(e.dma_start(out=st_sb[:], in_=stab[n, :, :]), 16)
            blk.sync(f_sync)
            scld_now = scldP[par].n
            yield "A"

            scev_base = scev.n
            tp_ids = []

            def f_tens(e):
                e.wait_ge(scldP[par].sem, scld_now)
                if it == 0:
                    e.wait_ge(scv.sem, SCV_INIT)
                    e.wait_ge(xev.sem, xev.n)
                for gidx in range(6):
                    pt = PTS[gidx % 2]
                    if gidx < 2:
                        e.wait_ge(scev.sem, scev_base)
                    else:
                        e.wait_ge(scev.sem, scev_base + gidx - 1)
                    srcb = (qsl[par], ksl[par], vsl[par])[gidx // 2]
                    g = gidx % 2 if False else (gidx & 1)
                    g = gidx % 2
                    half = (gidx % 2)
                    g2 = gidx - (gidx // 2) * 2
                    for j in range(8):
                        fblk = g2 * 8 + j
                        mm = e.transpose(pt[:, j * 128:(j + 1) * 128],
                                         srcb[:, fblk * 128:(fblk + 1) * 128],
                                         ident[:])
                    sctp.fire(mm)
                    tp_ids.append(sctp.n)
            blk.tensor(f_tens)

            def f_scal(e):
                if it >= 1:
                    e.wait_ge(scv.sem, scvtil_fin[it - 1])
                for gidx in range(6):
                    pt = PTS[gidx % 2]
                    dstb = (qnat, knat, vnat)[gidx // 2]
                    g2 = gidx - (gidx // 2) * 2
                    e.wait_ge(sctp.sem, tp_ids[gidx])
                    cp = e.copy(dstb[:, g2 * 1024:(g2 + 1) * 1024], pt[:, :])
                    scev.fire(cp)
            blk.scalar(f_scal)
            scs_prep = scev.n
            yield "B"

            hvv = lambda t, a, z: t[:, :].rearrange("p (h x) -> p h x", h=H)[:, :, a:z]
            ctr_ap = ct_sb[:, :].rearrange("p (h x) -> p h x", h=H)
            str_ap = st_sb[:, :].rearrange("p (h x) -> p h x", h=H)

            def f_vec(e):
                e.wait_ge(scev.sem, scs_prep)
                for src, dst, tmp in ((qnat, qrp, krp), (knat, krp, qnat)):
                    e.tensor_tensor(hvv(tmp, 0, 32), hvv(src, 32, 64), str_ap, ALU.mult)
                    e.tensor_tensor(hvv(tmp, 32, 64), hvv(src, 32, 64), ctr_ap, ALU.mult)
                    e.tensor_tensor(hvv(dst, 0, 32), hvv(src, 0, 32), ctr_ap, ALU.mult)
                    e.tensor_tensor(hvv(dst, 32, 64), hvv(src, 0, 32), str_ap, ALU.mult)
                    e.tensor_sub(hvv(dst, 0, 32), hvv(dst, 0, 32), hvv(tmp, 0, 32))
                    e.tensor_add(hvv(dst, 32, 64), hvv(dst, 32, 64), hvv(tmp, 32, 64))
                scv.fire(e.engine_nop())
            blk.vector(f_vec)
            scv_rope = scv.n
            scvrope_fin[it] = scv_rope

            def f_scal2(e):
                e.wait_ge(scv.sem, scv_rope)
                scs.fire(e.activation(qnat[:], qrp[:], AF.Square))
            blk.scalar(f_scal2)
            scs_q2 = scs.n

            def f_vec2(e):
                e.wait_ge(scs.sem, scs_q2)
                scv.fire(e.reduce_sum(
                    rsums[:, 0:32],
                    qnat[:, :].rearrange("p (h x) -> p h x", h=H), axis=AX))
            blk.vector(f_vec2)
            scv_r1 = scv.n

            def f_scal3(e):
                e.wait_ge(scv.sem, scv_r1)
                scs.fire(e.activation(qnat[:], krp[:], AF.Square))
                scs.fire(e.activation(rsq2[:, 0:32], rsums[:, 0:32], AF.Sqrt,
                                      bias=EPS))
            blk.scalar(f_scal3)
            scs_k2 = scs.n

            def f_vec3(e):
                e.wait_ge(scs.sem, scs_k2)
                scv.fire(e.reduce_sum(
                    rsums[:, 32:64],
                    qnat[:, :].rearrange("p (h x) -> p h x", h=H), axis=AX))
            blk.vector(f_vec3)
            scv_r2 = scv.n

            def f_scal4(e):
                e.wait_ge(scv.sem, scv_r2)
                scs.fire(e.activation(rsq2[:, 32:64], rsums[:, 32:64], AF.Sqrt,
                                      bias=EPS))
            blk.scalar(f_scal4)
            scs_k3 = scs.n
            yield "C"

            def f_vec4(e):
                e.wait_ge(scs.sem, scs_k3)
                e.wait_ge(scldP[par].sem, scld_now)
                e.reciprocal(rsums[:, 0:32], rsq2[:, 0:32])
                scv.fire(e.reciprocal(rsums[:, 32:64], rsq2[:, 32:64]))
                e.wait_ge(scv.sem, scv.n)
                e.tensor_copy(colbF[par][:], colb[par][:])
                e.tensor_tensor(sclq[:], rsums[:, 0:32], colbF[par][:, 64:96],
                                ALU.mult)
                for h in range(H):
                    e.tensor_scalar(qtil[:, h * 64:(h + 1) * 64],
                                    qrp[:, h * 64:(h + 1) * 64],
                                    sclq[:, h:h + 1], None, ALU.mult)
                    e.tensor_scalar(ktil[:, h * 64:(h + 1) * 64],
                                    krp[:, h * 64:(h + 1) * 64],
                                    rsums[:, 32 + h:33 + h], None, ALU.mult)
                    e.tensor_scalar(vb_bf[:, h * 64:(h + 1) * 64],
                                    vnat[:, h * 64:(h + 1) * 64],
                                    colbF[par][:, h:h + 1], None, ALU.mult)
                scv.fire(e.engine_nop())
            blk.vector(f_vec4)
            scv_til = scv.n
            scvtil_fin[it] = scv_til

            scev_base2 = scev.n
            tp2_ids = []

            def f_tens2(e):
                e.wait_ge(scv.sem, scv_til)
                for gidx in range(4):
                    pt = PTS[gidx % 2]
                    if gidx < 2:
                        e.wait_ge(scev.sem, scev_base2)
                    else:
                        e.wait_ge(scev.sem, scev_base2 + gidx - 1)
                    srcb = (qtil, ktil)[gidx // 2]
                    g2 = gidx - (gidx // 2) * 2
                    for j in range(8):
                        fblk = g2 * 8 + j
                        mm = e.transpose(pt[:, j * 128:(j + 1) * 128],
                                         srcb[:, fblk * 128:(fblk + 1) * 128],
                                         ident[:])
                    sctp.fire(mm)
                    tp2_ids.append(sctp.n)
            blk.tensor(f_tens2)

            def f_scal5(e):
                for gidx in range(4):
                    pt = PTS[gidx % 2]
                    dstb = (qT2, kT2)[gidx // 2]
                    g2 = gidx - (gidx // 2) * 2
                    e.wait_ge(sctp.sem, tp2_ids[gidx])
                    cp = e.copy(dstb[:, g2 * 1024:(g2 + 1) * 1024], pt[:, :])
                    scev.fire(cp)
            blk.scalar(f_scal5)
            sctp_fin[it] = sctp.n
            scs_t2 = scev.n

            def f_syncT(e):
                if it >= 1:
                    e.wait_ge(dmm.sem, dmm_fin[it - 1])
                sc = scldP[par]
                for dstt, srct, prow in ((T_A, crow3, 32), (T_B, crow3, 0),
                                         (T_C, clnb3, 0)):
                    sc.fire(e.dma_start(
                        out=dstt[prow:prow + 3, :]
                        .rearrange("j (h c) -> j h c", h=H),
                        in_=srct[b:b + 1, :, :, n:n + 1, :]
                        .rearrange("o j h m c -> (o j) h (m c)")), 16)
            blk.sync(f_syncT)
            scldT_now = scldP[par].n
            yield "D"

            for gp2 in range(4):
                pair = (2 * gp2, 2 * gp2 + 1)
                gidx0 = it * 8

                def hs_of(grp):
                    return [4 * grp + j for j in range(4)]

                for grp in pair:
                    bs = 3 * (grp % 2)
                    sl = grp % 2

                    def f_d(e, grp=grp, bs=bs):
                        if grp == 0:
                            e.wait_ge(scldP[par].sem, scldT_now)
                            e.wait_ge(scev.sem, scs_t2)
                            if it == 0:
                                e.wait_ge(gcev.sem, GC_DONE)
                        if scan_stage < 7:
                            for c_ in (emk, nbld, rbld, sev, aev, oev, supd,
                                       omm, smm, slv, amm, ymm):
                                e.wait_ge(c_.sem, c_.n)
                        for bank in (bs, bs + 1, bs + 2):
                            guard(e, bank)
                        for j, h in enumerate(hs_of(grp)):
                            hsl = slice(h * 128, (h + 1) * 128)
                            js = slice(j * 128, (j + 1) * 128)
                            e.matmul(PS[bs][:, js], T_A[0:35, hsl], T_B[0:35, hsl],
                                     start=True, stop=True)
                            e.matmul(PS[bs + 1][:, js], T_A[0:35, hsl],
                                     T_C[0:35, hsl], start=True, stop=True)
                            mm = e.matmul(PS[bs + 2][:, js], T_C[0:35, hsl],
                                          T_A[0:35, hsl], start=True, stop=True)
                        dmm.fire(mm)
                    blk.tensor(f_d)
                dmm_pair = {pair[0]: dmm.n - 1, pair[1]: dmm.n}

                emk_clamp = {}
                for grp in pair:
                    bs = 3 * (grp % 2)
                    sl = grp % 2

                    def f_cl(e, grp=grp, bs=bs, sl=sl):
                        e.wait_ge(dmm.sem, dmm_pair[grp])
                        e.wait_ge(nbld.sem, nbld_fin.get(gidx0 + grp - 2, 0))
                        e.scalar_tensor_tensor(EPs[sl][:], PS[bs][:, :], 0.0,
                                               masksr[:, 512:1024], ALU.min,
                                               ALU.add)
                        e.scalar_tensor_tensor(ENs[sl][:], PS[bs + 1][:, :], 0.0,
                                               masksr[:, 0:512], ALU.min, ALU.add)
                        emk.fire(e.scalar_tensor_tensor(
                            ENns[sl][:], PS[bs + 2][:, :], 0.0,
                            masksr[:, 1024:1536], ALU.min, ALU.add))
                    blk.vector(f_cl)
                    emk_clamp[grp] = emk.n

                emk_exp = {}
                for grp in pair:
                    sl = grp % 2

                    def f_ex(e, grp=grp, sl=sl):
                        e.wait_ge(emk.sem, emk_clamp[grp])
                        e.activation(EPs[sl][:], EPs[sl][:], AF.Exp)
                        e.activation(ENs[sl][:], ENs[sl][:], AF.Exp)
                        emk.fire(e.activation(ENns[sl][:], ENns[sl][:], AF.Exp))
                    blk.scalar(f_ex)
                    emk_exp[grp] = emk.n

                gmm_pair = {}
                ymm_pair = {}
                for grp in pair:
                    bs = 3 * (grp % 2)

                    def f_g(e, grp=grp, bs=bs):
                        e.wait_ge(emk.sem, emk_clamp[grp])
                        if it >= 2:
                            e.wait_ge(supd.sem, supd_fin[it - 2])
                        for j, h in enumerate(hs_of(grp)):
                            hp, tip = h // 2, h % 2
                            js = slice(j * 128, (j + 1) * 128)
                            kslc = kT2[tip * 64:(tip + 1) * 64,
                                       hp * 128:(hp + 1) * 128]
                            qslc = qT2[tip * 64:(tip + 1) * 64,
                                       hp * 128:(hp + 1) * 128]
                            if tip == 0:
                                e.matmul(PS[bs][:, js], kslc, kslc, start=True,
                                         stop=True, tile_position=(0, 0))
                                mm = e.matmul(PS[bs + 1][:, js], kslc, qslc,
                                              start=True, stop=True,
                                              tile_position=(0, 0))
                            else:
                                kA = kT2[64:128, hp * 128:hp * 128 + 64]
                                kB = kT2[64:128, hp * 128 + 64:(hp + 1) * 128]
                                e.matmul(PS[bs][0:64, js], kA, kslc, start=True,
                                         stop=True, tile_position=(64, 0))
                                e.matmul(PS[bs][64:128, js], kB, kslc,
                                         start=True, stop=True,
                                         tile_position=(64, 64))
                                e.matmul(PS[bs + 1][0:64, js], kA, qslc,
                                         start=True, stop=True,
                                         tile_position=(64, 0))
                                mm = e.matmul(PS[bs + 1][64:128, js], kB, qslc,
                                              start=True, stop=True,
                                              tile_position=(64, 64))
                        gmm.fire(mm)
                        for j, h in enumerate(hs_of(grp)):
                            hp, tip = h // 2, h % 2
                            kslc = kT2[tip * 64:(tip + 1) * 64,
                                       hp * 128:(hp + 1) * 128]
                            sslc = S_bf[tip * 64:(tip + 1) * 64,
                                        (b * 16 + hp) * 64:(b * 16 + hp + 1) * 64]
                            if tip == 0:
                                mm = e.matmul(PS[bs + 2][:, j * 64:(j + 1) * 64],
                                              kslc, sslc, start=True, stop=True,
                                              tile_position=(0, 0))
                            else:
                                kA = kT2[64:128, hp * 128:hp * 128 + 64]
                                kB = kT2[64:128, hp * 128 + 64:(hp + 1) * 128]
                                e.matmul(PS[bs + 2][0:64, j * 64:(j + 1) * 64],
                                         kA, sslc, start=True, stop=True,
                                         tile_position=(64, 0))
                                mm = e.matmul(PS[bs + 2][64:128,
                                                         j * 64:(j + 1) * 64],
                                              kB, sslc, start=True, stop=True,
                                              tile_position=(64, 64))
                        ymm.fire(mm)
                    blk.tensor(f_g)
                    gmm_pair[grp] = gmm.n
                    ymm_pair[grp] = ymm.n

                for grp in pair:
                    bs = 3 * (grp % 2)
                    sl = grp % 2

                    def f_n(e, grp=grp, bs=bs, sl=sl):
                        e.wait_ge(emk.sem, emk_exp[grp])
                        e.wait_ge(gmm.sem, gmm_pair[grp])
                        e.wait_ge(omm.sem, omm_fin.get(gidx0 + grp - 2, 0))
                        for j, h in enumerate(hs_of(grp)):
                            e.tensor_scalar(kpr[:, h * 64:(h + 1) * 64],
                                            ktil[:, h * 64:(h + 1) * 64],
                                            EPs[sl][:, j * 128 + 127:j * 128 + 128],
                                            None, ALU.mult)
                        e.tensor_tensor(trsS[sl][0][:], ENs[sl][:], PS[bs][:, :],
                                        ALU.mult)
                        e.tensor_tensor(natS[sl][0][:], ENns[sl][:], PS[bs][:, :],
                                        ALU.mult)
                        nbld.fire(e.tensor_tensor(PTbs[sl][:], EPs[sl][:],
                                                  PS[bs + 1][:, :], ALU.mult))
                    blk.vector(f_n)
                    nbld_fin[gidx0 + grp] = nbld.n

                for grp in pair:
                    bs = 3 * (grp % 2)
                    sl = grp % 2

                    def f_r(e, grp=grp, bs=bs, sl=sl):
                        e.wait_ge(ymm.sem, ymm_pair[grp])
                        e.wait_ge(amm.sem, amm_fin.get(gidx0 + grp - 2, 0))
                        for j, h in enumerate(hs_of(grp)):
                            e.scalar_tensor_tensor(
                                Rps[sl][0][:, j * 64:(j + 1) * 64],
                                PS[bs + 2][:, j * 64:(j + 1) * 64],
                                colbF[par][:, 32 + h:33 + h],
                                vb_bf[:, h * 64:(h + 1) * 64], ALU.mult, ALU.add)
                        rbld.fire(e.engine_nop())
                    blk.vector(f_r)
                rbld_pair = {pair[0]: rbld.n - 1, pair[1]: rbld.n}

                ev_prev = {g: None for g in pair}
                for lvl in range(7):
                    slv_lvl = {}
                    for grp in pair:
                        bs = 3 * (grp % 2)
                        sl = grp % 2
                        rcur = Rps[sl][lvl % 2]
                        tcur = trsS[sl][lvl % 2]
                        ncur = natS[sl][lvl % 2]

                        def f_lt(e, grp=grp, bs=bs, sl=sl, lvl=lvl, rcur=rcur,
                                 tcur=tcur, ncur=ncur):
                            if lvl == 0:
                                e.wait_ge(nbld.sem, nbld_fin[gidx0 + grp])
                                e.wait_ge(rbld.sem, rbld_pair[grp])
                            else:
                                e.wait_ge(sev.sem, ev_prev[grp][0])
                                e.wait_ge(aev.sem, ev_prev[grp][1])
                            for j in range(4):
                                js128 = slice(j * 128, (j + 1) * 128)
                                js64 = slice(j * 64, (j + 1) * 64)
                                if lvl == 0:
                                    mm = e.matmul(PS[bs + 2][:, js64],
                                                  tcur[:, js128], rcur[:, js64],
                                                  start=(j == 0), stop=(j == 3))
                                else:
                                    e.matmul(PS[bs + 2][:, js64], tcur[:, js128],
                                             rcur[:, js64], start=(j == 0),
                                             stop=False)
                                    mm = e.matmul(PS[bs + 2][:, js64], ident[:],
                                                  rcur[:, js64], start=False,
                                                  stop=(j == 3))
                            amm.fire(mm)
                            if lvl < 6:
                                for j in range(4):
                                    js = slice(j * 128, (j + 1) * 128)
                                    mm = e.matmul(PS[bs][:, js], ncur[:, js],
                                                  tcur[:, js], start=True,
                                                  stop=True)
                                slv.fire(mm)
                                if lvl < 5:
                                    for j in range(4):
                                        js = slice(j * 128, (j + 1) * 128)
                                        mm = e.matmul(PS[bs + 1][:, js],
                                                      tcur[:, js], ncur[:, js],
                                                      start=True, stop=True)
                                    slv.fire(mm)
                        blk.tensor(f_lt)
                        slv_lvl[grp] = (slv.n, amm.n)
                    for grp in pair:
                        bs = 3 * (grp % 2)
                        sl = grp % 2
                        rcur = Rps[sl][lvl % 2]
                        rnxt = Rps[sl][1 - lvl % 2]

                        def f_le_v(e, grp=grp, bs=bs, sl=sl, lvl=lvl,
                                   rcur=rcur, rnxt=rnxt):
                            e.wait_ge(amm.sem, slv_lvl[grp][1])
                            if lvl == 0:
                                aev.fire(e.scalar_tensor_tensor(
                                    rnxt[:], PS[bs + 2][:, 0:256], -1.0,
                                    rcur[:], ALU.mult, ALU.add))
                            elif lvl == 6:
                                aev.fire(e.tensor_copy(Ubs[sl][:],
                                                       PS[bs + 2][:, 0:256]))
                            else:
                                aev.fire(e.tensor_copy(rnxt[:],
                                                       PS[bs + 2][:, 0:256]))
                        blk.vector(f_le_v)
                        aev_l = aev.n

                        def f_le_s(e, grp=grp, bs=bs, sl=sl, lvl=lvl):
                            if lvl < 6:
                                e.wait_ge(slv.sem, slv_lvl[grp][0])
                                sev.fire(e.copy(trsS[sl][1 - lvl % 2][:],
                                                PS[bs][:, :]))
                        blk.scalar(f_le_s)

                        def f_le_v2(e, grp=grp, bs=bs, sl=sl, lvl=lvl):
                            if lvl < 5:
                                e.wait_ge(slv.sem, slv_lvl[grp][0])
                                sev.fire(e.tensor_copy(natS[sl][1 - lvl % 2][:],
                                                       PS[bs + 1][:, :]))
                        blk.vector(f_le_v2)
                        ev_prev[grp] = (sev.n, aev_l)
                for grp in pair:
                    amm_fin[gidx0 + grp] = amm.n
                aev_u = {g: ev_prev[g][1] for g in pair}

                omm_pair = {}
                for grp in pair:
                    bs = 3 * (grp % 2)
                    sl = grp % 2

                    def f_o(e, grp=grp, bs=bs, sl=sl):
                        e.wait_ge(aev.sem, aev_u[grp])
                        e.matmul(PS[bs + 2][:, 0:256], zrow[0:1, 0:128],
                                 zrow[0:1, 0:256], start=True, stop=False)
                        for pj in range(2):
                            hp = 2 * grp + pj
                            for tip in range(2):
                                j = 2 * pj + tip
                                e.matmul(PS[bs + 2][tip * 64:(tip + 1) * 64,
                                                    pj * 128:(pj + 1) * 128],
                                         Ubs[sl][:, j * 64:(j + 1) * 64],
                                         PTbs[sl][:, j * 128:(j + 1) * 128],
                                         start=False, stop=False,
                                         tile_position=(0, tip * 64))
                            for tip in range(2):
                                sslc = S_bf[tip * 64:(tip + 1) * 64,
                                            (b * 16 + hp) * 64:
                                            (b * 16 + hp + 1) * 64]
                                qslc = qT2[tip * 64:(tip + 1) * 64,
                                           hp * 128:(hp + 1) * 128]
                                mm = e.matmul(PS[bs + 2][tip * 64:(tip + 1) * 64,
                                                         pj * 128:(pj + 1) * 128],
                                              sslc, qslc, start=False,
                                              stop=(pj == 1 and tip == 1),
                                              tile_position=(tip * 64, tip * 64))
                        omm.fire(mm)
                    blk.tensor(f_o)
                    omm_pair[grp] = omm.n
                    omm_fin[gidx0 + grp] = omm.n

                oev_pair = {}
                for grp in pair:
                    bs = 3 * (grp % 2)

                    def f_oe(e, grp=grp, bs=bs):
                        e.wait_ge(omm.sem, omm_pair[grp])
                        if it >= 2:
                            e.wait_ge(ostP[par].sem, ost_fin[it - 2])
                        oev.fire(e.tensor_copy(
                            ostg[par][:, grp * 256:(grp + 1) * 256],
                            PS[bs + 2][:, 0:256]))
                    blk.vector(f_oe)
                    oev_pair[grp] = oev.n

                smm_pair = {}
                for grp in pair:
                    bs = 3 * (grp % 2)
                    sl = grp % 2

                    def f_ds(e, grp=grp, bs=bs, sl=sl):
                        e.wait_ge(nbld.sem, nbld_fin[gidx0 + grp])
                        e.wait_ge(supd.sem, supd.n)
                        for pj in range(2):
                            for tip in range(2):
                                j = 2 * pj + tip
                                h = 4 * grp + j
                                mm = e.matmul(PS[bs][tip * 64:(tip + 1) * 64,
                                                     pj * 64:(pj + 1) * 64],
                                              kpr[:, h * 64:(h + 1) * 64],
                                              Ubs[sl][:, j * 64:(j + 1) * 64],
                                              start=(pj == 0 and tip == 0),
                                              stop=(pj == 1 and tip == 1),
                                              tile_position=(0, tip * 64))
                        smm.fire(mm)
                    blk.tensor(f_ds)
                    smm_pair[grp] = smm.n

                for grp in pair:
                    bs = 3 * (grp % 2)

                    def f_su(e, grp=grp, bs=bs):
                        e.wait_ge(smm.sem, smm_pair[grp])
                        e.wait_ge(omm.sem, omm_pair[grp])
                        for pj in range(2):
                            hp = 2 * grp + pj
                            col = (b * 16 + hp)
                            gccol = GCm[:, b * 256 + hp * 16 + n:
                                        b * 256 + hp * 16 + n + 1]
                            e.scalar_tensor_tensor(
                                S_sb[:, col * 64:(col + 1) * 64],
                                S_sb[:, col * 64:(col + 1) * 64], gccol,
                                PS[bs][:, pj * 64:(pj + 1) * 64], ALU.mult,
                                ALU.add)
                            supd.fire(e.tensor_copy(
                                S_bf[:, col * 64:(col + 1) * 64],
                                S_sb[:, col * 64:(col + 1) * 64]))
                    blk.vector(f_su)

                for grp in pair:
                    bs = 3 * (grp % 2)
                    ps_guard[bs] = (supd, supd.n)
                    ps_guard[bs + 1] = (sev, ev_prev[grp][0])
                    ps_guard[bs + 2] = (oev, oev_pair[grp])

                if b == 0 and pair[0] == 0 and any(
                        k in tap_d for k in ("escan", "uscan", "sscan", "qktl")):
                    def f_tap(e):
                        if "escan" in tap_d:
                            e.wait_ge(nbld.sem, nbld_fin[gidx0 + 0])
                            tapE.fire(e.dma_start(
                                out=tap_d["escan"][:, n * 128:(n + 1) * 128],
                                in_=PTbs[0][:, 128:256]), 16)
                        if "uscan" in tap_d:
                            e.wait_ge(aev.sem, aev_u[0])
                            tapE.fire(e.dma_start(
                                out=tap_d["uscan"][:, n * 64:(n + 1) * 64],
                                in_=Ubs[0][:, 64:128]), 16)
                        if "sscan" in tap_d:
                            e.wait_ge(oev.sem, oev_pair[0])
                            tapE.fire(e.dma_start(
                                out=tap_d["sscan"][:, n * 256:(n + 1) * 256],
                                in_=ostg[par][:, 0:256]), 16)
                        if "qktl" in tap_d:
                            e.wait_ge(scev.sem, scs_t2)
                            tapE.fire(e.dma_start(
                                out=tap_d["qktl"][:, n * 256:n * 256 + 128],
                                in_=qT2[:, 0:128]), 16)
                            tapE.fire(e.dma_start(
                                out=tap_d["qktl"][:, n * 256 + 128:(n + 1) * 256],
                                in_=kT2[:, 0:128]), 16)
                    blk.gpsimd(f_tap)
                    tapE_after = tapE.n

                    def f_gate(e):
                        e.wait_ge(tapE.sem, tapE_after)
                    blk.vector(f_gate)
                if gp2 == 3 or not scan_core:
                    supd_fin[it] = supd.n
                    dmm_fin[it] = dmm.n if scan_core else 0
                yield f"P{gp2}"

            if not scan_core:
                supd_fin[it] = supd.n
                dmm_fin[it] = 0

            def f_gp_o(e):
                if not scan_core or scan_stage < 6:
                    return
                e.wait_ge(oev.sem, oev.n)
                dst = ott[:, :].rearrange(
                    "(hp tip dv) (bb nn i) -> (tip dv) hp bb nn i",
                    hp=16, tip=2, bb=NB, i=CH)[:, :, b:b + 1, n:n + 1, :]
                ostP[par].fire(e.dma_start(
                    out=dst.rearrange("p hp bb nn i -> p hp (bb nn i)"),
                    in_=ostg[par][:, :].rearrange("p (hp i) -> p hp i", hp=16)), 16)
            blk.gpsimd(f_gp_o)
            ost_fin[it] = ostP[par].n
            yield "F"

    gens = [_scan_gen(n_, b_, j) for j, (n_, b_) in enumerate(_sc_pairs)]

    def adv(j, upto):
        g, done = gens[j]
        if done or g is None:
            return
        while True:
            try:
                tag = next(g)
            except StopIteration:
                gens[j] = (None, True)
                return
            if tag == upto:
                return

    gens = [(g, False) for g in gens]
    NIT = len(_sc_pairs)
    if NIT:
        adv(0, "D")
    for j in range(NIT):
        adv(j, "P0")
        if j + 1 < NIT:
            adv(j + 1, "B")
        adv(j, "P1")
        if j + 1 < NIT:
            adv(j + 1, "C")
        adv(j, "P3")
        if j + 1 < NIT:
            adv(j + 1, "D")
        adv(j, "F")
    for j in range(NIT):
        adv(j, "ZZZ")

    def drain_sc(e):
        e.wait_ge(ostP[0].sem, ostP[0].n)
        e.wait_ge(ostP[1].sem, ostP[1].n)
        e.wait_ge(tapE.sem, tapE.n)
    blk.gpsimd(drain_sc)
    barrier()
    if "ott" in tap_d:
        def f(e):
            for r in range(0, D, 512):
                tapw.fire(e.dma_start(out=tap_d["ott"][r:r + 512, :],
                                      in_=ott[r:r + 512, :]), 16)
        blk.gpsimd(f)
        barrier()

    if phase_limit == 4:
        stack_b.close()
        ctx.close()
        return nc
    stack_b.close()

    stack_c = ExitStack()

    def sbc(name, shape, dt):
        return stack_c.enter_context(nc.sbuf_tensor(name, shape, dt))

    P3D = 4
    otb = [sbc(f"otb{i}", [128, 512], F32) for i in range(P3D)]
    vtb = [sbc(f"vtb{i}", [128, 512], BF16) for i in range(P3D)]
    gtb = [sbc(f"gtb{i}", [128, 512], BF16) for i in range(P3D)]
    o2b = [sbc(f"o2b{i}", [128, 512], F32) for i in range(P3D)]
    sqb = [sbc(f"sqb{i}", [128, 512], BF16) for i in range(P3D)]
    rmsb = [sbc(f"rmsb{i}", [2, 512], F32) for i in range(P3D)]
    rrb = [sbc(f"rrb{i}", [2, 512], BF16) for i in range(P3D)]
    rrf = [sbc(f"rrf{i}", [2, 512], F32) for i in range(P3D)]
    o5b = [sbc(f"o5b{i}", [128, 512], F32) for i in range(P3D)]
    ofb = [sbc(f"ofb{i}", [128, 512], BF16) for i in range(P3D)]

    h_of = {}
    h_rms = {}
    h_bc = {}
    h_st = {}

    for ti3 in range(128):
        ft = ti3 // 8
        tk = ti3 % 8
        par = ti3 % P3D

        def f_sync(e):
            if ti3 >= P3D:
                e.wait_ge(p3v.sem, h_of[ti3 - P3D])
            rs = slice(ft * 128, (ft + 1) * 128)
            cs = slice(tk * 512, (tk + 1) * 512)
            p3ldP[par % 2].fire(e.dma_start(out=otb[par][:], in_=ott[rs, cs]), 16)
            p3ldP[par % 2].fire(e.dma_start(out=vtb[par][:], in_=vt[rs, cs]), 16)
            p3ldP[par % 2].fire(e.dma_start(out=gtb[par][:], in_=gt[rs, cs]), 16)
        blk.gpsimd(f_sync)
        p3ld_now = p3ldP[par % 2].n

        def f_v1(e):
            e.wait_ge(p3ldP[par % 2].sem, p3ld_now)
            p3v.fire(e.scalar_tensor_tensor(o2b[par][:], vtb[par][:],
                                            dbc[:, ft:ft + 1], otb[par][:],
                                            ALU.mult, ALU.add))
        blk.vector(f_v1)
        p3v_o2 = p3v.n

        def f_s1(e):
            e.wait_ge(p3v.sem, p3v_o2)
            p3s.fire(e.activation(sqb[par][:], o2b[par][:], AF.Square))
        blk.scalar(f_s1)
        p3s_sq = p3s.n

        def f_t1(e):
            e.wait_ge(p3s.sem, p3s_sq)
            if ti3 >= 2:
                e.wait_ge(p3s.sem, h_rms[ti3 - 2])
            slot = 4 + (ti3 % 2)
            p3m.fire(e.matmul(PS[slot][0:2, :], indbd_s[:], sqb[par][:],
                              start=True, stop=True))
        blk.tensor(f_t1)
        p3m_s = p3m.n

        def f_s2(e):
            e.wait_ge(p3m.sem, p3m_s)
            slot = 4 + (ti3 % 2)
            p3s.fire(e.activation(rmsb[par][:], PS[slot][0:2, :], AF.Sqrt,
                                  scale=1.0 / DH, bias=EPS))
        blk.scalar(f_s2)
        h_rms[ti3] = p3s.n

        def f_v2(e):
            e.wait_ge(p3s.sem, h_rms[ti3])
            p3v.fire(e.reciprocal(rrf[par][:], rmsb[par][:]))
            e.wait_ge(p3v.sem, p3v.n)
            p3v.fire(e.tensor_copy(rrb[par][:], rrf[par][:]))
        blk.vector(f_v2)
        p3v_rr = p3v.n

        def f_t2(e):
            e.wait_ge(p3v.sem, p3v_rr)
            if ti3 >= 2:
                e.wait_ge(p3v.sem, h_bc[ti3 - 2])
            slot = 2 + (ti3 % 2)
            p3m.fire(e.matmul(PS[slot][:, :], ind2bf_s[:], rrb[par][:],
                              start=True, stop=True))
        blk.tensor(f_t2)
        p3m_b = p3m.n

        def f_v3(e):
            e.wait_ge(p3m.sem, p3m_b)
            slot = 2 + (ti3 % 2)
            e.tensor_tensor(o2b[par][:], o2b[par][:], PS[slot][:, :], ALU.mult)
            p3v.fire(e.tensor_scalar(o2b[par][:], o2b[par][:], onwcol[:, 0:1],
                                     None, ALU.mult))
        blk.vector(f_v3)
        h_bc[ti3] = p3v.n

        def f_s3(e):
            e.wait_ge(p3v.sem, h_bc[ti3])
            p3s.fire(e.activation(o5b[par][:], o2b[par][:], AF.Silu))
        blk.scalar(f_s3)
        p3s_o5 = p3s.n

        def f_v4(e):
            e.wait_ge(p3s.sem, p3s_o5)
            if ti3 >= P3D:
                e.wait_ge(p3stP[par % 2].sem, h_st[ti3 - P3D])
            p3v.fire(e.tensor_tensor(ofb[par][:], o5b[par][:], gtb[par][:],
                                     ALU.mult))
        blk.vector(f_v4)
        h_of[ti3] = p3v.n

        def f_gp(e):
            e.wait_ge(p3v.sem, h_of[ti3])
            rs = slice(ft * 128, (ft + 1) * 128)
            cs = slice(tk * 512, (tk + 1) * 512)
            p3stP[par % 2].fire(e.dma_start(out=ofin[rs, cs], in_=ofb[par][:]), 16)
        blk.gpsimd(f_gp)
        h_st[ti3] = p3stP[par % 2].n

    def drain_p3(e):
        e.wait_ge(p3stP[0].sem, p3stP[0].n)
        e.wait_ge(p3stP[1].sem, p3stP[1].n)
    blk.gpsimd(drain_p3)
    barrier()
    if "ofin" in tap_d:
        def f(e):
            for r in range(0, D, 512):
                tapw.fire(e.dma_start(out=tap_d["ofin"][r:r + 512, :],
                                      in_=ofin[r:r + 512, :]), 16)
        blk.gpsimd(f)
        barrier()

    if phase_limit == 5:
        stack_c.close()
        ctx.close()
        return nc
    stack_c.close()

    stack_d = ExitStack()

    def sbd(name, shape, dt):
        return stack_d.enter_context(nc.sbuf_tensor(name, shape, dt))

    woq = sbd("woq", [128, 16 * 512], BF16)
    wdq = sbd("wdq", [128, 64 * 512], BF16)
    ofl = [sbd(f"ofl{i}", [128, 16 * 128], BF16) for i in range(2)]
    htl = [sbd(f"htl{i}", [128, 64 * 128], BF16) for i in range(2)]
    xres = [sbd(f"xres{i}", [128, 512], F32) for i in range(2)]
    outb = [sbd(f"outb{i}", [128, 512], F32) for i in range(2)]

    h4_mm = {}
    h4_ev = {}
    h4_st = {}

    for nq in range(4):
        def f_syncw(e):
            if nq > 0:
                e.wait_ge(p4mm.sem, p4mm.n)
            e.wait_ge(wcastC["dn"].sem, CASTDONE["dn"])
            e.wait_ge(wcastC["o"].sem, CASTDONE["o"])
            for dblk in range(16):
                p4wl.fire(e.dma_start(
                    out=woq[:, dblk * 512:(dblk + 1) * 512],
                    in_=w_bf["o"][nq * 512:(nq + 1) * 512,
                                  dblk * 128:(dblk + 1) * 128],
                    transpose=True), 16)
            for iblk in range(64):
                p4wl.fire(e.dma_start(
                    out=wdq[:, iblk * 512:(iblk + 1) * 512],
                    in_=w_bf["dn"][nq * 512:(nq + 1) * 512,
                                   iblk * 128:(iblk + 1) * 128],
                    transpose=True), 16)
        blk.sync(f_syncw)
        p4wl_now = p4wl.n

        for tg in range(32):
            par = tg % 2
            tok0 = tg * 128
            gi = nq * 32 + tg

            def f_synca(e):
                if gi >= 2:
                    pg = (nq, tg - 2) if tg >= 2 else (nq - 1, 30 + tg)
                    e.wait_ge(p4mm.sem, h4_mm[pg])
                    e.wait_ge(p4ev.sem, h4_ev[pg])
                p4alP[par].fire(e.dma_start(
                    out=ofl[par][:, :].rearrange("p (a c) -> p a c", a=16),
                    in_=ofin[:, tok0:tok0 + 128]
                    .rearrange("(a p) c -> p a c", p=128)), 16)
                for a2 in range(2):
                    p4alP[par].fire(e.dma_start(
                        out=htl[par][:, a2 * 4096:(a2 + 1) * 4096]
                        .rearrange("p (a c) -> p a c", a=32),
                        in_=ht[a2 * 4096:(a2 + 1) * 4096, tok0:tok0 + 128]
                        .rearrange("(a p) c -> p a c", p=128)), 16)
                p4alP[par].fire(e.dma_start(
                    out=xres[par][:],
                    in_=x4[tok0:tok0 + 128, nq * 512:(nq + 1) * 512]), 16)
            blk.sync(f_synca)
            p4al_now = p4alP[par].n

            def f_tens(e):
                e.wait_ge(p4alP[par].sem, p4al_now)
                e.wait_ge(p4wl.sem, p4wl_now)
                slot = gi % 6
                if gi >= 6:
                    e.wait_ge(p4ev.sem, h4_ev[(nq, tg - 6) if tg >= 6
                                              else (nq - 1, 26 + tg)])
                first = True
                for dblk in range(16):
                    e.matmul(PS[slot][:, :],
                             ofl[par][:, dblk * 128:(dblk + 1) * 128],
                             woq[:, dblk * 512:(dblk + 1) * 512],
                             start=first, stop=False)
                    first = False
                for iblk in range(64):
                    mm = e.matmul(PS[slot][:, :],
                                  htl[par][:, iblk * 128:(iblk + 1) * 128],
                                  wdq[:, iblk * 512:(iblk + 1) * 512],
                                  start=False, stop=(iblk == 63))
                p4mm.fire(mm)
            blk.tensor(f_tens)
            h4_mm[(nq, tg)] = p4mm.n

            def f_vec(e):
                e.wait_ge(p4mm.sem, h4_mm[(nq, tg)])
                slot = gi % 6
                if gi >= 2:
                    pg = (nq, tg - 2) if tg >= 2 else (nq - 1, 30 + tg)
                    e.wait_ge(outwP[pg[1] % 2].sem, h4_st[pg])
                p4ev.fire(e.tensor_tensor(outb[par][:], xres[par][:],
                                          PS[slot][:, :], ALU.add))
            blk.vector(f_vec)
            h4_ev[(nq, tg)] = p4ev.n

            def f_gp(e):
                e.wait_ge(p4ev.sem, h4_ev[(nq, tg)])
                outwP[par].fire(e.dma_start(
                    out=out4[tok0:tok0 + 128, nq * 512:(nq + 1) * 512],
                    in_=outb[par][:]), 16)
            blk.gpsimd(f_gp)
            h4_st[(nq, tg)] = outwP[par].n

    def f_final(e):
        e.wait_ge(outwP[0].sem, outwP[0].n)
        e.wait_ge(outwP[1].sem, outwP[1].n)
    blk.gpsimd(f_final)
    barrier()

    stack_d.close()
    ctx.close()
    return nc


def kernel(**inputs):
    perm = _head_perm()
    consts = host_consts()

    def f32(a):
        return np.ascontiguousarray(np.asarray(a, np.float32))

    base = dict(
        wq=f32(inputs["L0_attn_Wq"])[perm], wk=f32(inputs["L0_attn_Wk"])[perm],
        wv=f32(inputs["L0_attn_Wv"]), wg=f32(inputs["L0_attn_Wg"]),
        wo=f32(inputs["L0_attn_Wo"]),
        wbgk=np.ascontiguousarray(np.concatenate(
            [f32(inputs["L0_attn_Wb"]), f32(inputs["L0_attn_Wgk"])], 0)),
        wgu=f32(inputs["L0_ffn_w_gate_up"]), wdn=f32(inputs["L0_ffn_w_down"]),
        cwq=np.ascontiguousarray(f32(inputs["L0_attn_wq_conv"])[perm]),
        cwk=np.ascontiguousarray(f32(inputs["L0_attn_wk_conv"])[perm]),
        cwv=f32(inputs["L0_attn_wv_conv"]),
        cbq=np.ascontiguousarray(f32(inputs["L0_attn_bq_conv"])[perm]),
        cbk=np.ascontiguousarray(f32(inputs["L0_attn_bk_conv"])[perm]),
        cbv=f32(inputs["L0_attn_bv_conv"]),
        alog=f32(inputs["L0_attn_A_log"]), dtb=f32(inputs["L0_attn_dt_bias"]),
        bbb=f32(inputs["L0_attn_bb"]), bgkb=f32(inputs["L0_attn_bgk"]),
        dvec=f32(inputs["L0_attn_D"]), onw=f32(inputs["L0_attn_onorm_w"]),
        **consts,
    )
    x = f32(inputs["L0_attn_x"])
    in_maps = []
    for c in range(NCORES):
        m = dict(base)
        m["x4"] = np.ascontiguousarray(x[c * NB:(c + 1) * NB].reshape(TOK, D))
        in_maps.append(m)

    nc = build_program()
    res = run_bass_kernel_spmd(nc, in_maps, core_ids=list(range(NCORES)))
    outs = [r["out4"].reshape(NB, T, D) for r in res.results]
    return np.concatenate(outs, axis=0)
